# version 02
# baseline (speedup 1.0000x reference)
import numpy as np
import concourse.bass as bass
import concourse.mybir as mybir
from concourse.bass_utils import run_bass_kernel_spmd

F32 = mybir.dt.float32
AF = mybir.ActivationFunctionType
ALU = mybir.AluOpType
AX = mybir.AxisListType

D = 1024
C = D // 128
R = 256
RC = R // 128
T = 4096
TT = 512
NTILE = T // TT
NE = 8
EPS = 1e-5
BIG = 1e30

ENGINES = ("tensor", "vector", "scalar", "sync", "gpsimd")


def build_nc():
    nc = bass.Bass("TRN2", target_bir_lowering=False)

    dram = {}
    def din(name, shape):
        dram[name] = nc.dram_tensor(name, shape, F32, kind="ExternalInput").ap()
        return dram[name]

    XF = din("XF", [D, T])
    WqT = din("WqT", [D, D]); WkT = din("WkT", [D, D]); WoT = din("WoT", [D, D])
    Vb = din("Vb", [D, R]); Wb = din("Wb", [D, R])
    Vt = din("Vt", [D, R]); Wt = din("Wt", [D, R]); Xt = din("Xt", [D, R])
    UbT = din("UbT", [R, D]); UtT = din("UtT", [R, D])
    BIASB = din("BIASB", [128, C]); BIAST = din("BIAST", [128, C])
    ALPHA = din("ALPHA", [1, 1])
    GWT = din("GWT", [D, NE]); GBIAS = din("GBIAS", [1, NE])
    W1T = din("W1T", [NE * D, D]); W3T = din("W3T", [NE * D, D]); W2T = din("W2T", [NE * D, D])
    SW1T = din("SW1T", [D, 2 * D]); SW3T = din("SW3T", [D, 2 * D]); SW2T = din("SW2T", [2 * D, D])
    IC = din("IC", [1, T])
    IDENT = din("IDENT", [128, 128])
    SELE = din("SELE", [NE, NE * 128])
    YF = nc.dram_tensor("YF", [D, T], F32, kind="ExternalOutput").ap()

    def sb(name, shape, dt=F32):
        return nc.alloc_sbuf_tensor(name, shape, dt)

    xbuf = sb("xbuf", [128, C, TT])
    qbuf = sb("qbuf", [128, C, TT])
    kbuf = sb("kbuf", [128, C, TT])
    r1buf = sb("r1buf", [128, C, TT])
    h1buf = sb("h1buf", [128, C, TT])
    h3buf = kbuf
    ab = sb("ab", [128, RC, TT]); at = sb("at", [128, RC, TT])
    wb_s = sb("wb_s", [128, RC, TT]); wt_s = sb("wt_s", [128, RC, TT]); uz = sb("uz", [128, RC, TT])
    cb = sb("cb", [128, RC, TT]); ct = sb("ct", [128, RC, TT]); cz = sb("cz", [128, RC, TT])
    pb = sb("pb", [128, RC, TT]); pt = sb("pt", [128, RC, TT])
    wbuf = sb("wbuf", [128, 10240])
    icb = sb("icb", [128, TT])
    geb = sb("geb", [128, TT])
    rb = sb("rb", [128, TT]); mb = sb("mb", [128, TT])
    g_sb = sb("g_sb", [NE, TT]); gf_sb = sb("gf_sb", [NE, TT])
    sc_tm = sb("sc_tm", [128, 32]); ex_tm = sb("ex_tm", [128, 32])
    vv = sb("vv", [128, 32]); ge1 = sb("ge1", [128, 32]); bigb = sb("bigb", [128, 32])
    keep = sb("keep", [128, 32]); gts = sb("gts", [128, 32])
    mx4 = sb("mx4", [128, 4]); nmx4 = sb("nmx4", [128, 4]); sm4 = sb("sm4", [128, 4])
    rs4 = sb("rs4", [128, 4]); m14 = sb("m14", [128, 4]); m24 = sb("m24", [128, 4])
    stats = sb("stats", [1, 4096])
    ic_t = sb("ic_t", [1, TT])
    ident = sb("ident", [128, 128])
    ones_col = sb("ones_col", [128, 1]); ones_row = sb("ones_row", [1, 128])
    ones_scan = sb("ones_scan", [128, TT])
    fenceb = sb("fenceb", [128, 1])
    alpha_sb = sb("alpha_sb", [1, 1]); alphaB = sb("alphaB", [128, 1])
    biasb_sb = sb("biasb_sb", [128, C]); biast_sb = sb("biast_sb", [128, C])
    bvec = sb("bvec", [128, C]); tmp8 = sb("tmp8", [128, C])
    gb_row = sb("gb_row", [1, NE]); gbB = sb("gbB", [128, NE])
    sele_sb = sb("sele_sb", [NE, NE * 128])
    wob = sb("wob", [128, C])
    carry = sb("carry", [128, 6])

    ST_SUM, ST_SQ, ST_MEAN, ST_EX2, ST_MSQ, ST_VAR = 0, 512, 1024, 1536, 2048, 2560
    ST_RS0, ST_T1, ST_RSTD, ST_NMR = 3072, 3584, 0, 512

    ps = [nc.alloc_psum_tensor(f"ps{i}", [128, TT], F32) for i in range(8)]

    sems = {}
    for s in ("sLD", "sST", "sT", "sV", "sS", "sG"):
        sems[s] = nc.alloc_semaphore(s)
    cnt = {s: 0 for s in sems}
    stream = {e: [] for e in ENGINES}

    def op(eng, fn, inc=None, amt=1):
        if inc is None:
            stream[eng].append(fn)
            return None
        cnt[inc] += amt
        sem = sems[inc]
        def wrapped(e, fn=fn, sem=sem, amt=amt):
            fn(e).then_inc(sem, amt)
        stream[eng].append(wrapped)
        return cnt[inc]

    def wait(eng, s, val):
        if val is None or val <= 0:
            return
        assert val <= cnt[s], f"wait {s}>{cnt[s]} emitted on {eng} for {val}"
        sem = sems[s]
        stream[eng].append(lambda e, sem=sem, val=val: e.wait_ge(sem, val))

    def fence(eng="vector"):
        stream[eng].append(lambda e: e.memset(fenceb[:], 0.0))

    def vhard(fn):
        n = op("vector", fn, inc="sV")
        wait("vector", "sV", n)
        return n

    def dma(eng, out_ap, in_ap, inc, waits=()):
        for (s, v) in waits:
            wait(eng, s, v)
        return op(eng, lambda e, o=out_ap, i=in_ap: e.dma_start(out=o, in_=i), inc=inc, amt=16)

    def mm_phase(groups, waits=(), banks=None):
        if banks is None:
            banks = list(range(8))
        assert len(groups) <= len(banks), "split phases with >8 groups"
        w = {"sS": cnt["sS"], "sV": cnt["sV"]}
        for (s, v) in waits:
            w[s] = max(w.get(s, 0), v or 0)
        for s, v in w.items():
            wait("tensor", s, v)
        marks = []
        for gi, g in enumerate(groups):
            bank = ps[banks[gi % len(banks)]]
            out_ap = g["out"](bank) if "out" in g else bank[:]
            mmlist = g["mm"]
            n = len(mmlist)
            for j, (l, r) in enumerate(mmlist):
                if j == n - 1:
                    m = op("tensor",
                           lambda e, o=out_ap, l=l, r=r, st=(j == 0): e.matmul(o, l, r, start=st, stop=True),
                           inc="sT")
                else:
                    op("tensor",
                       lambda e, o=out_ap, l=l, r=r, st=(j == 0): e.matmul(o, l, r, start=st, stop=False))
            marks.append(m)
        return marks

    def evict(dst_ap, psum_ap, mark, eng="scalar"):
        wait(eng, "sT", mark)
        s = "sS" if eng == "scalar" else "sV"
        if eng == "scalar":
            return op(eng, lambda e, d=dst_ap, p=psum_ap: e.copy(d, p), inc=s)
        else:
            return op(eng, lambda e, d=dst_ap, p=psum_ap: e.tensor_copy(d, p), inc=s)

    wv = wbuf.ap()
    def wview(nchunk, width, off=0):
        return wv[:, off:off + nchunk * width].rearrange("p (c j) -> p c j", j=width)

    wv_big = wview(C, 1024)
    wv_gate = wview(C, NE)
    wv_r = [wview(RC, 1024, off=i * 2048) for i in range(2)]
    wv_5r = [wview(C, R, off=i * 2048) for i in range(5)]

    op("vector", lambda e: e.memset(ones_col[:], 1.0))
    op("vector", lambda e: e.memset(ones_row[:], 1.0))
    op("vector", lambda e: e.memset(ones_scan[:], 1.0))
    op("vector", lambda e: e.memset(carry[:], 0.0))
    v_pre = op("vector", lambda e: e.memset(fenceb[:], 0.0), inc="sV")

    d_id = dma("sync", ident[:], IDENT, inc="sLD")
    d_se = dma("sync", sele_sb[:], SELE, inc="sLD")
    d_al = dma("sync", alpha_sb[:], ALPHA, inc="sLD")
    d_bb = dma("sync", biasb_sb[:], BIASB, inc="sLD")
    d_bt = dma("sync", biast_sb[:], BIAST, inc="sLD")
    d_gb = dma("sync", gb_row[:], GBIAS, inc="sLD")

    mk = mm_phase([
        {"mm": [(ones_row[:], alpha_sb[:])], "out": lambda b: b[0:128, 0:1]},
        {"mm": [(ones_row[:], gb_row[:])], "out": lambda b: b[0:128, 0:NE]},
    ], waits=[("sLD", cnt["sLD"]), ("sV", v_pre)], banks=[0, 1])
    e_al = evict(alphaB[:], ps[0][0:128, 0:1], mk[0])
    e_gb = evict(gbB[:], ps[1][0:128, 0:NE], mk[1])
    wait("vector", "sS", e_al)
    vhard(lambda e: e.tensor_scalar(tmp8[:], biast_sb[:], alphaB[:], None, ALU.mult))
    v_bvec = vhard(lambda e: e.tensor_tensor(bvec[:], biasb_sb[:], tmp8[:], ALU.add))

    store_marks = []
    for tau in range(NTILE):
        tok = slice(tau * TT, (tau + 1) * TT)

        d_x = dma("sync", xbuf[:], XF.rearrange("(c p) t -> p c t", p=128)[:, :, tok],
                  inc="sLD", waits=[("sV", cnt["sV"])])

        d_w = dma("sync", wv_big[:, :, :], WqT.rearrange("(c p) j -> p c j", p=128),
                  inc="sLD", waits=[("sT", cnt["sT"])])
        marks = mm_phase(
            [{"mm": [(wv_big[:, k, m * 128:(m + 1) * 128], xbuf[:, k, :]) for k in range(C)]}
             for m in range(C)],
            waits=[("sLD", max(d_x, d_w))])
        if store_marks:
            wait("scalar", "sST", store_marks[-1])
        for m in range(C):
            evict(qbuf[:, m, :], ps[m][:], marks[m])

        d_w = dma("sync", wv_big[:, :, :], WkT.rearrange("(c p) j -> p c j", p=128),
                  inc="sLD", waits=[("sT", cnt["sT"])])
        marks = mm_phase(
            [{"mm": [(wv_big[:, k, m * 128:(m + 1) * 128], xbuf[:, k, :]) for k in range(C)]}
             for m in range(C)],
            waits=[("sLD", d_w)])
        for m in range(C):
            evict(kbuf[:, m, :], ps[m][:], marks[m])

        d5 = []
        wait("sync", "sT", cnt["sT"])
        for i, W in enumerate((Vb, Vt, Wb, Wt, Xt)):
            d5.append(dma("sync", wv_5r[i][:, :, :], W.rearrange("(c p) r -> p c r", p=128), inc="sLD"))
        proj = [(ab, 0, qbuf), (at, 1, qbuf), (wb_s, 2, kbuf), (wt_s, 3, kbuf), (uz, 4, xbuf)]
        groups = []
        outs = []
        for dstbuf, wi, src in proj:
            for m in range(RC):
                groups.append({"mm": [(wv_5r[wi][:, k, m * 128:(m + 1) * 128], src[:, k, :])
                                      for k in range(C)]})
                outs.append((dstbuf, m))
        for base in range(0, len(groups), 8):
            chunk = groups[base:base + 8]
            marks = mm_phase(chunk, waits=[("sLD", d5[-1])])
            for gi, mk in enumerate(marks):
                dstbuf, m = outs[base + gi]
                evict(dstbuf[:, m, :], ps[gi % 8][:], mk)
        ev_proj = cnt["sS"]

        wait("vector", "sS", ev_proj)
        scans = [(cb, wb_s, 0), (ct, wt_s, 2), (cz, uz, 4)]
        for dst, src, ci in scans:
            for rc_ in range(RC):
                op("vector", lambda e, d=dst, s=src, rc_=rc_, ci=ci:
                   e.tensor_tensor_scan(d[:, rc_, :], ones_scan[:], s[:, rc_, :],
                                        carry[:, ci + rc_:ci + rc_ + 1], ALU.mult, ALU.add),
                   inc="sV")
        wait("vector", "sV", cnt["sV"])
        for dst, src, ci in scans:
            for rc_ in range(RC):
                op("vector", lambda e, d=dst, rc_=rc_, ci=ci:
                   e.tensor_copy(carry[:, ci + rc_:ci + rc_ + 1], d[:, rc_, TT - 1:TT]), inc="sV")

        d_ict = dma("sync", ic_t[:], IC[0:1, tok], inc="sLD", waits=[("sT", cnt["sT"])])
        mkic = mm_phase([{"mm": [(ones_row[:], ic_t[:])]}],
                        waits=[("sV", v_bvec), ("sLD", d_ict)], banks=[7])
        e_ic = evict(icb[:], ps[7][:], mkic[0])

        wait("vector", "sS", e_ic)
        for rc_ in range(RC):
            op("vector", lambda e, rc_=rc_: e.tensor_tensor(uz[:, rc_, :], cz[:, rc_, :], icb[:], ALU.mult))
        for rc_ in range(RC):
            op("vector", lambda e, rc_=rc_: e.tensor_tensor(pb[:, rc_, :], ab[:, rc_, :], cb[:, rc_, :], ALU.mult))
        for rc_ in range(RC):
            op("vector", lambda e, rc_=rc_: e.tensor_tensor(pt[:, rc_, :], at[:, rc_, :], uz[:, rc_, :], ALU.mult))
        for rc_ in range(RC):
            op("vector", lambda e, rc_=rc_: e.tensor_tensor(pt[:, rc_, :], pt[:, rc_, :], ct[:, rc_, :], ALU.mult))
        v_prod = None
        for rc_ in range(RC):
            v_prod = op("vector", lambda e, rc_=rc_:
                        e.tensor_scalar(pt[:, rc_, :], pt[:, rc_, :], alphaB[:], None, ALU.mult), inc="sV")

        d_u1 = dma("sync", wv_r[0][:, :, :], UbT.rearrange("(c p) j -> p c j", p=128),
                   inc="sLD", waits=[("sT", cnt["sT"])])
        d_u2 = dma("sync", wv_r[1][:, :, :], UtT.rearrange("(c p) j -> p c j", p=128), inc="sLD")
        marks = mm_phase(
            [{"mm": [(wv_r[0][:, k, m * 128:(m + 1) * 128], pb[:, k, :]) for k in range(RC)]
                    + [(wv_r[1][:, k, m * 128:(m + 1) * 128], pt[:, k, :]) for k in range(RC)]}
             for m in range(C)],
            waits=[("sLD", d_u2), ("sV", v_prod), ("sS", cnt["sS"])])
        for m in range(C):
            evict(qbuf[:, m, :], ps[m][:], marks[m])

        d_wo = dma("sync", wv_big[:, :, :], WoT.rearrange("(c p) j -> p c j", p=128),
                   inc="sLD", waits=[("sT", cnt["sT"])])
        if tau == 0:
            wait("vector", "sS", cnt["sS"])
            mkb = mm_phase(
                [{"mm": [(wv_big[:, k, m * 128:(m + 1) * 128], bvec[:, k:k + 1]) for k in range(C)],
                  "out": (lambda m: (lambda b: b[0:128, m:m + 1]))(m)}
                 for m in range(C)],
                waits=[("sLD", d_wo), ("sV", v_bvec)], banks=[0] * C)
            e_wob = evict(wob[:], ps[0][0:128, 0:C], mkb[-1])
        marks = mm_phase(
            [{"mm": [(wv_big[:, k, m * 128:(m + 1) * 128], qbuf[:, k, :]) for k in range(C)]}
             for m in range(C)],
            waits=[("sLD", d_wo), ("sS", cnt["sS"])])
        for m in range(C):
            wait("vector", "sT", marks[m])
            op("vector", lambda e, m=m: e.tensor_tensor(r1buf[:, m, :], ps[m][:], icb[:], ALU.mult),
               inc="sV")
        for m in range(C):
            op("vector", lambda e, m=m:
               e.tensor_scalar(r1buf[:, m, :], r1buf[:, m, :], wob[:, m:m + 1], None, ALU.add))
        v_r1 = None
        for m in range(C):
            v_r1 = op("vector", lambda e, m=m:
                      e.tensor_tensor(r1buf[:, m, :], r1buf[:, m, :], xbuf[:, m, :], ALU.add), inc="sV")

        def layernorm(src, dst, v_src_ready):
            wait("scalar", "sV", v_src_ready)
            s_sq = op("scalar", lambda e: e.activation(kbuf[:], src[:], AF.Square), inc="sS")
            mks = mm_phase(
                [{"mm": [(ones_col[:], src[:, k, :]) for k in range(C)],
                  "out": lambda b: b[0:1, :]},
                 {"mm": [(ones_col[:], kbuf[:, k, :]) for k in range(C)],
                  "out": lambda b: b[0:1, :]}],
                waits=[("sV", v_src_ready), ("sS", s_sq)], banks=[0, 1])
            e_su = evict(stats[0:1, ST_SUM:ST_SUM + TT], ps[0][0:1, :], mks[0])
            e_sq = evict(stats[0:1, ST_SQ:ST_SQ + TT], ps[1][0:1, :], mks[1])
            st = lambda o: stats[0:1, o:o + TT]
            wait("vector", "sS", e_sq)
            vhard(lambda e: e.tensor_scalar(st(ST_MEAN), st(ST_SUM), 1.0 / D, None, ALU.mult))
            vhard(lambda e: e.tensor_scalar(st(ST_EX2), st(ST_SQ), 1.0 / D, None, ALU.mult))
            vhard(lambda e: e.tensor_tensor(st(ST_MSQ), st(ST_MEAN), st(ST_MEAN), ALU.mult))
            vhard(lambda e: e.tensor_tensor(st(ST_VAR), st(ST_EX2), st(ST_MSQ), ALU.subtract))
            vhard(lambda e: e.tensor_scalar(st(ST_VAR), st(ST_VAR), EPS, None, ALU.add))
            v_rs = vhard(lambda e: e.reciprocal(st(ST_RS0), st(ST_VAR)))
            wait("scalar", "sV", v_rs)
            s_r0 = op("scalar", lambda e: e.activation(st(ST_T1), st(ST_RS0), AF.Sqrt), inc="sS")
            wait("vector", "sS", s_r0)
            vhard(lambda e: e.tensor_tensor(st(ST_RSTD), st(ST_T1), st(ST_T1), ALU.mult))
            vhard(lambda e: e.tensor_tensor(st(ST_RSTD), st(ST_RSTD), st(ST_VAR), ALU.mult))
            vhard(lambda e: e.tensor_scalar(st(ST_RSTD), st(ST_RSTD), -0.5, 1.5, ALU.mult, ALU.add))
            vhard(lambda e: e.tensor_tensor(st(ST_RSTD), st(ST_RSTD), st(ST_T1), ALU.mult))
            vhard(lambda e: e.tensor_tensor(st(ST_NMR), st(ST_MEAN), st(ST_RSTD), ALU.mult))
            v_nm = vhard(lambda e: e.tensor_scalar(st(ST_NMR), st(ST_NMR), -1.0, None, ALU.mult))
            mkb2 = mm_phase(
                [{"mm": [(ones_row[:], st(ST_RSTD))]},
                 {"mm": [(ones_row[:], st(ST_NMR))]}],
                waits=[("sV", v_nm), ("sS", cnt["sS"])], banks=[2, 3])
            e_rb = evict(rb[:], ps[2][:], mkb2[0])
            e_mb = evict(mb[:], ps[3][:], mkb2[1])
            wait("vector", "sS", e_mb)
            for k in range(C):
                op("vector", lambda e, k=k: e.tensor_tensor(dst[:, k, :], src[:, k, :], rb[:], ALU.mult))
            v_last = None
            for k in range(C):
                v_last = op("vector", lambda e, k=k:
                            e.tensor_tensor(dst[:, k, :], dst[:, k, :], mb[:], ALU.add), inc="sV")
            return v_last

        v_y1 = layernorm(r1buf, xbuf, v_r1)

        d_gw = dma("sync", wv_gate[:, :, :], GWT.rearrange("(c p) j -> p c j", p=128),
                   inc="sLD", waits=[("sT", cnt["sT"])])
        mkg = mm_phase(
            [{"mm": [(wv_gate[:, k, :], xbuf[:, k, :]) for k in range(C)],
              "out": lambda b: b[0:NE, :]}],
            waits=[("sLD", d_gw), ("sV", v_y1), ("sS", cnt["sS"])], banks=[4])
        e_g = evict(g_sb[:], ps[4][0:NE, :], mkg[0])
        wait("tensor", "sS", e_g)
        tmarks = []
        for b4 in range(4):
            tmarks.append(op("tensor", lambda e, b4=b4:
                             e.transpose(ps[5][0:128, b4 * 8:(b4 + 1) * 8],
                                         g_sb[0:NE, b4 * 128:(b4 + 1) * 128],
                                         ident[0:NE, 0:NE]), inc="sT"))
        e_tm = evict(sc_tm[:], ps[5][0:128, 0:32], tmarks[-1])
        wait("vector", "sS", e_tm)
        S8 = lambda t_, b4: t_[:, b4 * 8:(b4 + 1) * 8]
        def step(fns):
            n = None
            for f in fns:
                n = op("vector", f, inc="sV")
            wait("vector", "sV", n)
        step([lambda e, b4=b4: e.tensor_reduce(mx4[:, b4:b4 + 1], S8(sc_tm, b4), AX.X, ALU.max)
              for b4 in range(4)])
        v_nm4 = vhard(lambda e: e.tensor_scalar(nmx4[:], mx4[:], -1.0, None, ALU.mult))
        wait("scalar", "sV", v_nm4)
        s_ex = None
        for b4 in range(4):
            s_ex = op("scalar", lambda e, b4=b4:
                      e.activation(S8(ex_tm, b4), S8(sc_tm, b4), AF.Exp, bias=nmx4[:, b4:b4 + 1]),
                      inc="sS")
        wait("vector", "sS", s_ex)
        step([lambda e, b4=b4: e.tensor_reduce(sm4[:, b4:b4 + 1], S8(ex_tm, b4), AX.X, ALU.add)
              for b4 in range(4)])
        vhard(lambda e: e.reciprocal(rs4[:], sm4[:]))
        step([lambda e, b4=b4:
              e.tensor_scalar(S8(ex_tm, b4), S8(ex_tm, b4), rs4[:, b4:b4 + 1], None, ALU.mult)
              for b4 in range(4)])
        step([lambda e, b4=b4: e.tensor_tensor(S8(vv, b4), S8(ex_tm, b4), gbB[:], ALU.add)
              for b4 in range(4)])
        step([lambda e, b4=b4: e.tensor_reduce(m14[:, b4:b4 + 1], S8(vv, b4), AX.X, ALU.max)
              for b4 in range(4)])
        step([lambda e, b4=b4:
              e.tensor_scalar(S8(ge1, b4), S8(vv, b4), m14[:, b4:b4 + 1], None, ALU.is_ge)
              for b4 in range(4)])
        step([lambda e, b4=b4:
              e.tensor_scalar(S8(bigb, b4), S8(ge1, b4), -BIG, None, ALU.mult)
              for b4 in range(4)])
        step([lambda e, b4=b4: e.tensor_tensor(S8(bigb, b4), S8(vv, b4), S8(bigb, b4), ALU.add)
              for b4 in range(4)])
        step([lambda e, b4=b4: e.tensor_reduce(m24[:, b4:b4 + 1], S8(bigb, b4), AX.X, ALU.max)
              for b4 in range(4)])
        step([lambda e, b4=b4:
              e.tensor_scalar(S8(keep, b4), S8(vv, b4), m24[:, b4:b4 + 1], None, ALU.is_ge)
              for b4 in range(4)])
        v_gt = None
        for b4 in range(4):
            v_gt = op("vector", lambda e, b4=b4:
                      e.tensor_tensor(S8(gts, b4), S8(ex_tm, b4), S8(keep, b4), ALU.mult), inc="sV")
        wait("vector", "sV", v_gt)
        wait("tensor", "sV", v_gt)
        wait("tensor", "sS", cnt["sS"])
        tmarks = []
        for b4 in range(4):
            tmarks.append(op("tensor", lambda e, b4=b4:
                             e.transpose(ps[6][0:NE, b4 * 128:(b4 + 1) * 128],
                                         S8(gts, b4), ident[0:128, 0:128]), inc="sT"))
        e_gf = evict(gf_sb[:], ps[6][0:NE, 0:TT], tmarks[-1])

        def h_phase(wsrc_dram, dstbuf):
            d_ = dma("sync", wv_big[:, :, :], wsrc_dram, inc="sLD", waits=[("sT", cnt["sT"])])
            mks = mm_phase(
                [{"mm": [(wv_big[:, k, m * 128:(m + 1) * 128], xbuf[:, k, :]) for k in range(C)]}
                 for m in range(C)],
                waits=[("sLD", d_), ("sS", cnt["sS"]), ("sV", cnt["sV"])])
            for m in range(C):
                evict(dstbuf[:, m, :], ps[m][:], mks[m])
            return cnt["sS"]

        def silu_mult():
            ev = cnt["sS"]
            wait("scalar", "sS", ev)
            s_si = op("scalar", lambda e: e.activation(h1buf[:], h1buf[:], AF.Silu), inc="sS")
            wait("vector", "sS", s_si)
            v_h = None
            for m in range(C):
                v_h = op("vector", lambda e, m=m:
                         e.tensor_tensor(h1buf[:, m, :], h1buf[:, m, :], h3buf[:, m, :], ALU.mult),
                         inc="sV")
            return v_h

        for half in range(2):
            h_phase(SW1T.rearrange("(c p) j -> p c j", p=128)[:, :, half * 1024:(half + 1) * 1024], h1buf)
            h_phase(SW3T.rearrange("(c p) j -> p c j", p=128)[:, :, half * 1024:(half + 1) * 1024], h3buf)
            v_h = silu_mult()
            d_2 = dma("sync", wv_big[:, :, :],
                      SW2T[half * 1024:(half + 1) * 1024, :].rearrange("(c p) j -> p c j", p=128),
                      inc="sLD", waits=[("sT", cnt["sT"])])
            mks = mm_phase(
                [{"mm": [(wv_big[:, k, m * 128:(m + 1) * 128], h1buf[:, k, :]) for k in range(C)]}
                 for m in range(C)],
                waits=[("sLD", d_2), ("sV", v_h), ("sS", cnt["sS"])])
            if half == 0:
                for m in range(C):
                    evict(r1buf[:, m, :], ps[m][:], mks[m])
            else:
                for m in range(C):
                    wait("vector", "sT", mks[m])
                    op("vector", lambda e, m=m:
                       e.tensor_tensor(r1buf[:, m, :], r1buf[:, m, :], ps[m][:], ALU.add),
                       inc="sV")

        for ex in range(NE):
            wait("tensor", "sS", e_gf)
            wait("tensor", "sV", cnt["sV"])
            mkge = mm_phase([{"mm": [(sele_sb[0:NE, ex * 128:(ex + 1) * 128], gf_sb[0:NE, :])]}], banks=[7])
            e_ge = evict(geb[:], ps[7][:], mkge[0])
            h_phase(W1T[ex * D:(ex + 1) * D, :].rearrange("(c p) j -> p c j", p=128), h1buf)
            h_phase(W3T[ex * D:(ex + 1) * D, :].rearrange("(c p) j -> p c j", p=128), h3buf)
            v_h = silu_mult()
            d_2 = dma("sync", wv_big[:, :, :],
                      W2T[ex * D:(ex + 1) * D, :].rearrange("(c p) j -> p c j", p=128),
                      inc="sLD", waits=[("sT", cnt["sT"])])
            mks = mm_phase(
                [{"mm": [(wv_big[:, k, m * 128:(m + 1) * 128], h1buf[:, k, :]) for k in range(C)]}
                 for m in range(C)],
                waits=[("sLD", d_2), ("sV", v_h), ("sS", max(cnt["sS"], e_ge))])
            for m in range(C):
                wait("vector", "sT", mks[m])
                op("vector", lambda e, m=m:
                   e.tensor_tensor(h3buf[:, m, :], ps[m][:], geb[:], ALU.mult), inc="sV")
            for m in range(C):
                op("vector", lambda e, m=m:
                   e.tensor_tensor(r1buf[:, m, :], r1buf[:, m, :], h3buf[:, m, :], ALU.add),
                   inc="sV")

        v_r2 = None
        for m in range(C):
            v_r2 = op("vector", lambda e, m=m:
                      e.tensor_tensor(r1buf[:, m, :], r1buf[:, m, :], xbuf[:, m, :], ALU.add),
                      inc="sV")
        v_out = layernorm(r1buf, qbuf, v_r2)

        wait("gpsimd", "sV", v_out)
        smark = dma("gpsimd", YF.rearrange("(c p) t -> p c t", p=128)[:, :, tok], qbuf[:], inc="sST")
        store_marks.append(smark)

    wait("gpsimd", "sST", 16 * NTILE)

    with nc.Block() as block:
        @block.sync
        def _(e):
            for fn in stream["sync"]:
                fn(e)

        @block.tensor
        def _(e):
            for fn in stream["tensor"]:
                fn(e)

        @block.vector
        def _(e):
            for fn in stream["vector"]:
                fn(e)

        @block.scalar
        def _(e):
            for fn in stream["scalar"]:
                fn(e)

        @block.gpsimd
        def _(e):
            for fn in stream["gpsimd"]:
                fn(e)

    return nc


def _sele():
    s = np.zeros((NE, NE * 128), dtype=np.float32)
    for e in range(NE):
        s[e, e * 128:(e + 1) * 128] = 1.0
    return s


def kernel(**inputs):
    g = {k: np.asarray(v) for k, v in inputs.items()}
    x = g["L0_attn_x"]
    B = x.shape[0]

    def cc(a):
        return np.ascontiguousarray(a.astype(np.float32, copy=False))

    shared = {
        "WqT": cc(g["L0_attn_Wq"].T), "WkT": cc(g["L0_attn_Wk"].T), "WoT": cc(g["L0_attn_Wo"].T),
        "Vb": cc(g["L0_attn_V_b"]), "Wb": cc(g["L0_attn_W_b"]),
        "Vt": cc(g["L0_attn_V_t"]), "Wt": cc(g["L0_attn_W_t"]), "Xt": cc(g["L0_attn_X_t"]),
        "UbT": cc(g["L0_attn_U_b"].T), "UtT": cc(g["L0_attn_U_t"].T),
        "BIASB": cc(g["L0_attn_bias_b"].reshape(C, 128).T),
        "BIAST": cc(g["L0_attn_bias_t"].reshape(C, 128).T),
        "ALPHA": cc(g["L0_attn_alpha"].reshape(1, 1)),
        "GWT": cc(g["L0_ffn_gate_w"].T),
        "GBIAS": cc(g["L0_ffn_gate_bias"].reshape(1, NE)),
        "W1T": cc(g["L0_ffn_w1"].transpose(0, 2, 1).reshape(NE * D, D)),
        "W3T": cc(g["L0_ffn_w3"].transpose(0, 2, 1).reshape(NE * D, D)),
        "W2T": cc(g["L0_ffn_w2"].transpose(0, 2, 1).reshape(NE * D, D)),
        "SW1T": cc(g["L0_ffn_sw1"].T), "SW3T": cc(g["L0_ffn_sw3"].T),
        "SW2T": cc(g["L0_ffn_sw2"].T),
        "IC": (1.0 / np.arange(1, T + 1, dtype=np.float64)).astype(np.float32).reshape(1, T),
        "IDENT": np.eye(128, dtype=np.float32),
        "SELE": _sele(),
    }
    in_maps = []
    for b in range(B):
        m = dict(shared)
        m["XF"] = cc(x[b].T)
        in_maps.append(m)

    nc = build_nc()
    res = run_bass_kernel_spmd(nc, in_maps, core_ids=list(range(B)))
    out = np.stack([np.asarray(r["YF"]).T for r in res.results])
    return np.ascontiguousarray(out.astype(np.float32))


# version 03
# speedup vs baseline: 1.5484x; 1.5484x over previous
import numpy as np
import concourse.bass as bass
import concourse.mybir as mybir
from concourse.bass_utils import run_bass_kernel_spmd

F32 = mybir.dt.float32
BF16 = mybir.dt.bfloat16
AF = mybir.ActivationFunctionType
ALU = mybir.AluOpType
AX = mybir.AxisListType

D = 1024
C = D // 128
R = 256
RC = R // 128
T = 4096
TT = 512
NTILE = T // TT
NE = 8
EPS = 1e-5
BIG = 1e30

ENGINES = ("tensor", "vector", "scalar", "sync", "gpsimd")


def build_nc():
    nc = bass.Bass("TRN2", target_bir_lowering=False)

    def din(name, shape, dt=F32):
        return nc.dram_tensor(name, shape, dt, kind="ExternalInput").ap()

    XF = din("XF", [D, T])
    WqT = din("WqT", [D, D]); WkT = din("WkT", [D, D]); WoT = din("WoT", [D, D])
    Vb = din("Vb", [D, R]); Wb = din("Wb", [D, R])
    Vt = din("Vt", [D, R]); Wt = din("Wt", [D, R]); Xt = din("Xt", [D, R])
    UbT = din("UbT", [R, D]); UtT = din("UtT", [R, D])
    BIASB = din("BIASB", [128, C]); BIAST = din("BIAST", [128, C])
    ALPHA = din("ALPHA", [1, 1])
    GWT = din("GWT", [D, NE]); GBIAS = din("GBIAS", [1, NE])
    W1T = din("W1T", [NE * D, D]); W3T = din("W3T", [NE * D, D]); W2T = din("W2T", [NE * D, D])
    SW1T = din("SW1T", [D, 2 * D]); SW3T = din("SW3T", [D, 2 * D]); SW2T = din("SW2T", [2 * D, D])
    IC = din("IC", [1, T])
    IDENT = din("IDENT", [128, 128])
    SELE = din("SELE", [NE, NE * 128])
    YF = nc.dram_tensor("YF", [D, T], F32, kind="ExternalOutput").ap()

    def scr(name, shape):
        return nc.dram_tensor(name, shape, BF16, kind="Internal").ap()
    W1HI = scr("W1HI", [NE * D, D]); W1LO = scr("W1LO", [NE * D, D])
    W3HI = scr("W3HI", [NE * D, D]); W3LO = scr("W3LO", [NE * D, D])
    W2HI = scr("W2HI", [NE * D, D]); W2LO = scr("W2LO", [NE * D, D])
    S1HI = scr("S1HI", [D, 2 * D]); S1LO = scr("S1LO", [D, 2 * D])
    S3HI = scr("S3HI", [D, 2 * D]); S3LO = scr("S3LO", [D, 2 * D])
    S2HI = scr("S2HI", [2 * D, D]); S2LO = scr("S2LO", [2 * D, D])

    def sb(name, shape, dt=F32):
        return nc.alloc_sbuf_tensor(name, shape, dt)

    xbuf = sb("xbuf", [128, C, TT])
    qbuf = sb("qbuf", [128, C, TT])
    kbuf = sb("kbuf", [128, C, TT])
    r1buf = sb("r1buf", [128, C, TT])
    h1buf = sb("h1buf", [128, C, TT])
    ab = sb("ab", [128, RC, TT]); at = sb("at", [128, RC, TT])
    wb_s = sb("wb_s", [128, RC, TT]); wt_s = sb("wt_s", [128, RC, TT]); uz = sb("uz", [128, RC, TT])
    cb = sb("cb", [128, RC, TT]); ct = sb("ct", [128, RC, TT]); cz = sb("cz", [128, RC, TT])
    pb = sb("pb", [128, RC, TT]); pt = sb("pt", [128, RC, TT])
    wbuf = sb("wbuf", [128, 2, 4096])
    icb = sb("icb", [128, TT])
    geb = sb("geb", [128, 2, TT])
    rb = sb("rb", [128, TT]); mb = sb("mb", [128, TT])
    tmpc = sb("tmpc", [128, 4, TT])
    g_sb = sb("g_sb", [NE, TT]); gf_sb = sb("gf_sb", [NE, TT])
    sc_tm = sb("sc_tm", [128, 32]); ex_tm = sb("ex_tm", [128, 32])
    vv = sb("vv", [128, 32]); ge1 = sb("ge1", [128, 32]); bigb = sb("bigb", [128, 32])
    keep = sb("keep", [128, 32]); gts = sb("gts", [128, 32])
    mx4 = sb("mx4", [128, 4]); nmx4 = sb("nmx4", [128, 4]); sm4 = sb("sm4", [128, 4])
    rs4 = sb("rs4", [128, 4]); m14 = sb("m14", [128, 4]); m24 = sb("m24", [128, 4])
    stats = sb("stats", [1, 2048])
    ic_t = sb("ic_t", [1, TT])
    ident = sb("ident", [128, 128])
    ones_col = sb("ones_col", [128, 1]); ones_row = sb("ones_row", [1, 128])
    ones_scan = sb("ones_scan", [128, TT], BF16)
    fenceb = sb("fenceb", [128, 1])
    alpha_sb = sb("alpha_sb", [1, 1]); alphaB = sb("alphaB", [128, 1])
    biasb_sb = sb("biasb_sb", [128, C]); biast_sb = sb("biast_sb", [128, C])
    bvec = sb("bvec", [128, C]); tmp8 = sb("tmp8", [128, C])
    gb_row = sb("gb_row", [1, NE]); gbB = sb("gbB", [128, NE])
    sele_sb = sb("sele_sb", [NE, NE * 128])
    wob = sb("wob", [128, C])
    carry = sb("carry", [128, 6])

    ST_A, ST_B, ST_C, ST_D_ = 0, 512, 1024, 1536

    ps = [nc.alloc_psum_tensor(f"ps{i}", [128, TT], F32) for i in range(8)]

    sems = {}
    for s_ in ("sLD", "sST", "sT", "sV", "sS"):
        sems[s_] = nc.alloc_semaphore(s_)
    cnt = {s_: 0 for s_ in sems}
    stream = {e: [] for e in ENGINES}
    bank_free = [("sS", 0) for _ in range(8)]

    def op(eng, fn, inc=None, amt=1):
        if inc is None:
            stream[eng].append(fn)
            return None
        cnt[inc] += amt
        sem = sems[inc]
        def wrapped(e, fn=fn, sem=sem, amt=amt):
            fn(e).then_inc(sem, amt)
        stream[eng].append(wrapped)
        return cnt[inc]

    def wait(eng, s_, val):
        if val is None or val <= 0:
            return
        assert val <= cnt[s_], f"wait {s_}>{cnt[s_]} on {eng} for {val}"
        sem = sems[s_]
        stream[eng].append(lambda e, sem=sem, val=val: e.wait_ge(sem, val))

    def waits_merged(eng, waits):
        w = {}
        for (s_, v) in waits:
            if v:
                w[s_] = max(w.get(s_, 0), v)
        for s_, v in w.items():
            wait(eng, s_, v)

    def vhard(fn):
        n = op("vector", fn, inc="sV")
        wait("vector", "sV", n)
        return n

    def dma(eng, out_ap, in_ap, inc, waits=()):
        waits_merged(eng, waits)
        return op(eng, lambda e, o=out_ap, i=in_ap: e.dma_start(out=o, in_=i), inc=inc, amt=16)

    def mm_phase(groups, waits=(), banks=None):
        if banks is None:
            banks = list(range(8))
        assert len(groups) <= len(banks)
        allw = list(waits) + [bank_free[banks[i]] for i in range(len(groups))]
        waits_merged("tensor", allw)
        marks = []
        for gi, g in enumerate(groups):
            bank = ps[banks[gi]]
            out_ap = g["out"](bank) if "out" in g else bank[:]
            mmlist = g["mm"]
            n = len(mmlist)
            for j, (l, r) in enumerate(mmlist):
                if j == n - 1:
                    mk = op("tensor",
                            lambda e, o=out_ap, l=l, r=r, st=(j == 0): e.matmul(o, l, r, start=st, stop=True),
                            inc="sT")
                else:
                    op("tensor",
                       lambda e, o=out_ap, l=l, r=r, st=(j == 0): e.matmul(o, l, r, start=st, stop=False))
            marks.append(mk)
        return marks

    def evict(dst_ap, bank, mark, psum_ap=None, eng="scalar", waits=()):
        waits_merged(eng, [("sT", mark)] + list(waits))
        p = ps[bank][:] if psum_ap is None else psum_ap
        if eng == "scalar":
            n = op(eng, lambda e, d=dst_ap, p=p: e.copy(d, p), inc="sS")
            bank_free[bank] = ("sS", n)
        else:
            n = op(eng, lambda e, d=dst_ap, p=p: e.tensor_copy(d, p), inc="sV")
            bank_free[bank] = ("sV", n)
        return n

    slot_f32 = [wbuf[:, s_, :] for s_ in range(2)]
    slot_bf = [wbuf[:, s_, :].bitcast(BF16) for s_ in range(2)]
    slot_use = [("sT", 0), ("sT", 0)]
    slot_ld = [0, 0]
    slot_next = [0]

    def wload(entries, extra_waits=()):
        s_ = slot_next[0]
        slot_next[0] = 1 - s_
        waits_merged("sync", [slot_use[s_]] + list(extra_waits))
        mkk = 0
        for dst_fn, src in entries:
            mkk = dma("sync", dst_fn(s_), src, inc="sLD")
        slot_ld[s_] = mkk
        return s_

    def slot_mark(s_, mark):
        slot_use[s_] = ("sT", mark)

    def v_f32(nchunk, width):
        return lambda s_, n=nchunk, w=width: slot_f32[s_][:, 0:n * w].rearrange("p (c j) -> p c j", j=w)

    def v_hi(s_):
        return slot_bf[s_][:, 0:4096].rearrange("p (c j) -> p c j", j=512)

    def v_lo(s_):
        return slot_bf[s_][:, 4096:8192].rearrange("p (c j) -> p c j", j=512)

    op("vector", lambda e: e.memset(ones_col[:], 1.0))
    op("vector", lambda e: e.memset(ones_row[:], 1.0))
    op("vector", lambda e: e.memset(ones_scan[:], 1.0))
    op("vector", lambda e: e.memset(carry[:], 0.0))
    v_pre = op("vector", lambda e: e.memset(fenceb[:], 0.0), inc="sV")

    dma("sync", ident[:], IDENT, inc="sLD")
    dma("sync", sele_sb[:], SELE, inc="sLD")
    dma("sync", alpha_sb[:], ALPHA, inc="sLD")
    dma("sync", biasb_sb[:], BIASB, inc="sLD")
    dma("sync", biast_sb[:], BIAST, inc="sLD")
    dma("sync", gb_row[:], GBIAS, inc="sLD")

    mk = mm_phase([
        {"mm": [(ones_row[:], alpha_sb[:])], "out": lambda b: b[0:128, 0:1]},
        {"mm": [(ones_row[:], gb_row[:])], "out": lambda b: b[0:128, 0:NE]},
    ], waits=[("sLD", cnt["sLD"]), ("sV", v_pre)], banks=[0, 1])
    e_al = evict(alphaB[:], 0, mk[0], psum_ap=ps[0][0:128, 0:1])
    e_gb = evict(gbB[:], 1, mk[1], psum_ap=ps[1][0:128, 0:NE])
    wait("vector", "sS", e_al)
    vhard(lambda e: e.tensor_scalar(tmp8[:], biast_sb[:], alphaB[:], None, ALU.mult))
    v_bvec = vhard(lambda e: e.tensor_tensor(bvec[:], biasb_sb[:], tmp8[:], ALU.add))

    kb16 = kbuf[:].rearrange("p c t -> p (c t)").bitcast(BF16)
    xb16 = xbuf[:].rearrange("p c t -> p (c t)").bitcast(BF16)
    out_regions = [(kb16[:, 0:4096], kb16[:, 4096:8192]),
                   (xb16[:, 0:4096], xb16[:, 4096:8192])]
    out_busy = [0, 0]

    pre_jobs = []
    def add_pre(src, hi, lo, rows, cols):
        pre_jobs.append((src.rearrange("(c p) j -> p c j", p=128),
                         hi.rearrange("(c p) j -> p c j", p=128),
                         lo.rearrange("(c p) j -> p c j", p=128), rows // 128, cols))

    add_pre(SW1T, S1HI, S1LO, D, 2 * D)
    add_pre(SW3T, S3HI, S3LO, D, 2 * D)
    add_pre(SW2T, S2HI, S2LO, 2 * D, D)
    for e_ in range(NE):
        add_pre(W1T[e_ * D:(e_ + 1) * D, :], W1HI[e_ * D:(e_ + 1) * D, :], W1LO[e_ * D:(e_ + 1) * D, :], D, D)
        add_pre(W3T[e_ * D:(e_ + 1) * D, :], W3HI[e_ * D:(e_ + 1) * D, :], W3LO[e_ * D:(e_ + 1) * D, :], D, D)
        add_pre(W2T[e_ * D:(e_ + 1) * D, :], W2HI[e_ * D:(e_ + 1) * D, :], W2LO[e_ * D:(e_ + 1) * D, :], D, D)

    pre_i = 0
    for (srcv, hiv, lov, nch, cols) in pre_jobs:
        per = max(1, 4096 // cols)
        for c0 in range(0, nch, per):
            n = min(per, nch - c0)
            fel = n * cols
            s_ = slot_next[0]; slot_next[0] = 1 - s_
            waits_merged("sync", [slot_use[s_]])
            dmk = dma("sync", slot_f32[s_][:, 0:fel].rearrange("p (c j) -> p c j", j=cols),
                      srcv[:, c0:c0 + n, :], inc="sLD")
            reg = pre_i % 2
            hi_ap = out_regions[reg][0][:, 0:fel]
            lo_ap = out_regions[reg][1][:, 0:fel]
            src_ap = slot_f32[s_][:, 0:fel]
            waits_merged("scalar", [("sLD", dmk), ("sST", out_busy[reg])])
            n_hi = op("scalar", lambda e, d=hi_ap, x=src_ap: e.copy(d, x), inc="sS")
            wait("vector", "sS", n_hi)
            n_lo = op("vector", lambda e, d=lo_ap, x=src_ap, h=hi_ap:
                      e.tensor_tensor(d, x, h, ALU.subtract), inc="sV")
            slot_use[s_] = ("sV", n_lo)
            waits_merged("gpsimd", [("sS", n_hi), ("sV", n_lo)])
            dma("gpsimd", hiv[:, c0:c0 + n, :], hi_ap.rearrange("p (c j) -> p c j", j=cols), inc="sST")
            smk = dma("gpsimd", lov[:, c0:c0 + n, :], lo_ap.rearrange("p (c j) -> p c j", j=cols), inc="sST")
            out_busy[reg] = smk
            pre_i += 1
    pre_done = cnt["sST"]
    wait("scalar", "sST", pre_done)
    wait("vector", "sST", pre_done)

    store_marks = []
    v_acc_last = 0
    for tau in range(NTILE):
        tok = slice(tau * TT, (tau + 1) * TT)

        xw = [("sV", cnt["sV"])]
        if tau == 0:
            xw.append(("sST", pre_done))
        d_x = dma("sync", xbuf[:], XF.rearrange("(c p) t -> p c t", p=128)[:, :, tok],
                  inc="sLD", waits=xw)

        ev_q = ev_k = 0
        for (Wsrc, dstbuf) in ((WqT, qbuf), (WkT, kbuf)):
            for half in range(2):
                s_ = wload([(v_f32(C, 512),
                             Wsrc.rearrange("(c p) j -> p c j", p=128)[:, :, half * 512:(half + 1) * 512])])
                mks = mm_phase(
                    [{"mm": [(v_f32(C, 512)(s_)[:, k, m * 128:(m + 1) * 128], xbuf[:, k, :])
                             for k in range(C)]} for m in range(4)],
                    waits=[("sLD", slot_ld[s_]), ("sLD", d_x)],
                    banks=[4 * half + i for i in range(4)])
                slot_mark(s_, mks[-1])
                if dstbuf is qbuf and half == 0 and store_marks:
                    wait("scalar", "sST", store_marks[-1])
                for i, m in enumerate(range(4 * half, 4 * half + 4)):
                    n_ = evict(dstbuf[:, m, :], 4 * half + i, mks[i])
                if dstbuf is qbuf:
                    ev_q = n_
                else:
                    ev_k = n_

        proj = [(Vb, ab, qbuf, ev_q), (Vt, at, qbuf, ev_q), (Wb, wb_s, kbuf, ev_k),
                (Wt, wt_s, kbuf, ev_k), (Xt, uz, xbuf, 0)]
        pb_ix = 0
        for (Wsrc, dstbuf, srcbuf, ev_need) in proj:
            s_ = wload([(v_f32(C, R), Wsrc.rearrange("(c p) r -> p c r", p=128))])
            mks = mm_phase(
                [{"mm": [(v_f32(C, R)(s_)[:, k, m * 128:(m + 1) * 128], srcbuf[:, k, :])
                         for k in range(C)]} for m in range(RC)],
                waits=[("sLD", slot_ld[s_]), ("sS", ev_need)],
                banks=[pb_ix % 8, (pb_ix + 1) % 8])
            slot_mark(s_, mks[-1])
            for m in range(RC):
                evict(dstbuf[:, m, :], (pb_ix + m) % 8, mks[m])
            pb_ix += 2
        ev_proj = cnt["sS"]

        wait("vector", "sS", ev_proj)
        scans = [(cb, wb_s, 0), (ct, wt_s, 2), (cz, uz, 4)]
        for dst, src, ci in scans:
            for rc_ in range(RC):
                op("vector", lambda e, d=dst, s=src, rc_=rc_, ci=ci:
                   e.tensor_tensor_scan(d[:, rc_, :], ones_scan[:], s[:, rc_, :],
                                        carry[:, ci + rc_:ci + rc_ + 1], ALU.mult, ALU.add),
                   inc="sV")
        wait("vector", "sV", cnt["sV"])
        for dst, src, ci in scans:
            for rc_ in range(RC):
                op("vector", lambda e, d=dst, rc_=rc_, ci=ci:
                   e.tensor_copy(carry[:, ci + rc_:ci + rc_ + 1], d[:, rc_, TT - 1:TT]), inc="sV")

        d_ict = dma("sync", ic_t[:], IC[0:1, tok], inc="sLD", waits=[("sT", cnt["sT"])])
        mkic = mm_phase([{"mm": [(ones_row[:], ic_t[:])]}],
                        waits=[("sV", v_bvec), ("sLD", d_ict)], banks=[7])
        e_ic = evict(icb[:], 7, mkic[0])

        wait("vector", "sS", e_ic)
        for rc_ in range(RC):
            op("vector", lambda e, rc_=rc_: e.tensor_tensor(uz[:, rc_, :], cz[:, rc_, :], icb[:], ALU.mult))
        for rc_ in range(RC):
            op("vector", lambda e, rc_=rc_: e.tensor_tensor(pb[:, rc_, :], ab[:, rc_, :], cb[:, rc_, :], ALU.mult))
        for rc_ in range(RC):
            op("vector", lambda e, rc_=rc_: e.tensor_tensor(pt[:, rc_, :], at[:, rc_, :], uz[:, rc_, :], ALU.mult))
        for rc_ in range(RC):
            op("vector", lambda e, rc_=rc_: e.tensor_tensor(pt[:, rc_, :], pt[:, rc_, :], ct[:, rc_, :], ALU.mult))
        v_prod = None
        for rc_ in range(RC):
            v_prod = op("vector", lambda e, rc_=rc_:
                        e.tensor_scalar(pt[:, rc_, :], pt[:, rc_, :], alphaB[:], None, ALU.mult), inc="sV")

        ubv = lambda s2: slot_f32[s2][:, 0:2048].rearrange("p (c j) -> p c j", j=1024)
        utv = lambda s2: slot_f32[s2][:, 2048:4096].rearrange("p (c j) -> p c j", j=1024)
        s_ = wload([(lambda s2: ubv(s2), UbT.rearrange("(c p) j -> p c j", p=128)),
                    (lambda s2: utv(s2), UtT.rearrange("(c p) j -> p c j", p=128))])
        mks = mm_phase(
            [{"mm": [(ubv(s_)[:, k, m * 128:(m + 1) * 128], pb[:, k, :]) for k in range(RC)]
                    + [(utv(s_)[:, k, m * 128:(m + 1) * 128], pt[:, k, :]) for k in range(RC)]}
             for m in range(C)],
            waits=[("sLD", slot_ld[s_]), ("sV", v_prod)])
        slot_mark(s_, mks[-1])
        for m in range(C):
            evict(qbuf[:, m, :], m, mks[m])
        ev_s = cnt["sS"]

        v_r1 = None
        for half in range(2):
            s_ = wload([(v_f32(C, 512),
                         WoT.rearrange("(c p) j -> p c j", p=128)[:, :, half * 512:(half + 1) * 512])])
            if tau == 0:
                mkb = mm_phase(
                    [{"mm": [(v_f32(C, 512)(s_)[:, k, m * 128:(m + 1) * 128], bvec[:, k:k + 1])
                             for k in range(C)],
                      "out": (lambda mm_: (lambda b: b[0:128, mm_:mm_ + 1]))(m)}
                     for m in range(4)],
                    waits=[("sLD", slot_ld[s_]), ("sV", v_bvec)], banks=[half] * 4)
                evict(wob[:, 4 * half:4 * half + 4], half, mkb[-1],
                      psum_ap=ps[half][0:128, 0:4])
            mks = mm_phase(
                [{"mm": [(v_f32(C, 512)(s_)[:, k, m * 128:(m + 1) * 128], qbuf[:, k, :])
                         for k in range(C)]} for m in range(4)],
                waits=[("sLD", slot_ld[s_]), ("sS", ev_s)],
                banks=[4 * half + i for i in range(4)])
            slot_mark(s_, mks[-1])
            for i, m in enumerate(range(4 * half, 4 * half + 4)):
                wait("vector", "sT", mks[i])
                n_ = op("vector", lambda e, m=m, b=4 * half + i:
                        e.tensor_tensor(r1buf[:, m, :], ps[b][:], icb[:], ALU.mult), inc="sV")
                bank_free[4 * half + i] = ("sV", n_)
            for m in range(4 * half, 4 * half + 4):
                op("vector", lambda e, m=m:
                   e.tensor_scalar(r1buf[:, m, :], r1buf[:, m, :], wob[:, m:m + 1], None, ALU.add))
            for m in range(4 * half, 4 * half + 4):
                v_r1 = op("vector", lambda e, m=m:
                          e.tensor_tensor(r1buf[:, m, :], r1buf[:, m, :], xbuf[:, m, :], ALU.add),
                          inc="sV")

        def layernorm(src, dst, v_src_ready, sq_tensor_ok=0):
            wait("scalar", "sV", v_src_ready)
            wait("scalar", "sT", sq_tensor_ok)
            s_sq = op("scalar", lambda e: e.activation(kbuf[:], src[:], AF.Square), inc="sS")
            mks2 = mm_phase(
                [{"mm": [(ones_col[:], src[:, k, :]) for k in range(C)], "out": lambda b: b[0:1, :]},
                 {"mm": [(ones_col[:], kbuf[:, k, :]) for k in range(C)], "out": lambda b: b[0:1, :]}],
                waits=[("sV", v_src_ready), ("sS", s_sq)], banks=[0, 1])
            evict(stats[0:1, ST_A:ST_A + TT], 0, mks2[0], psum_ap=ps[0][0:1, :])
            e_sq = evict(stats[0:1, ST_B:ST_B + TT], 1, mks2[1], psum_ap=ps[1][0:1, :])
            st = lambda o: stats[0:1, o:o + TT]
            wait("vector", "sS", e_sq)
            vhard(lambda e: e.tensor_scalar(st(ST_C), st(ST_A), 1.0 / D, None, ALU.mult))
            vhard(lambda e: e.tensor_scalar(st(ST_B), st(ST_B), 1.0 / D, EPS, ALU.mult, ALU.add))
            vhard(lambda e: e.tensor_tensor(st(ST_A), st(ST_C), st(ST_C), ALU.mult))
            vhard(lambda e: e.tensor_tensor(st(ST_B), st(ST_B), st(ST_A), ALU.subtract))
            v_rs = vhard(lambda e: e.reciprocal(st(ST_A), st(ST_B)))
            wait("scalar", "sV", v_rs)
            s_r0 = op("scalar", lambda e: e.activation(st(ST_D_), st(ST_A), AF.Sqrt), inc="sS")
            wait("vector", "sS", s_r0)
            vhard(lambda e: e.tensor_tensor(st(ST_C), st(ST_C), st(ST_D_), ALU.mult))
            v_nm = vhard(lambda e: e.tensor_scalar(st(ST_C), st(ST_C), -1.0, None, ALU.mult))
            mkb2 = mm_phase(
                [{"mm": [(ones_row[:], st(ST_D_))]}, {"mm": [(ones_row[:], st(ST_C))]}],
                waits=[("sV", v_nm)], banks=[2, 3])
            evict(rb[:], 2, mkb2[0])
            e_mb = evict(mb[:], 3, mkb2[1])
            wait("vector", "sS", e_mb)
            for k in range(C):
                op("vector", lambda e, k=k: e.tensor_tensor(dst[:, k, :], src[:, k, :], rb[:], ALU.mult))
            v_last = None
            for k in range(C):
                v_last = op("vector", lambda e, k=k:
                            e.tensor_tensor(dst[:, k, :], dst[:, k, :], mb[:], ALU.add), inc="sV")
            return v_last, mks2[1]

        v_y1, ln1_sq_mark = layernorm(r1buf, xbuf, v_r1)

        qb16 = qbuf[:].rearrange("p c t -> p (c t)").bitcast(BF16)
        y1h = qb16[:, 0:4096].rearrange("p (c j) -> p c j", j=512)
        y1l = qb16[:, 4096:8192].rearrange("p (c j) -> p c j", j=512)
        v_init = None
        for k in range(C):
            v_init = op("vector", lambda e, k=k:
                        e.tensor_copy(r1buf[:, k, :], xbuf[:, k, :]), inc="sV")
        s_y1h = op("scalar", lambda e: e.copy(y1h[:, :, :], xbuf[:]), inc="sS")
        wait("vector", "sS", s_y1h)
        v_y1l = None
        for k in range(C):
            v_y1l = op("vector", lambda e, k=k:
                       e.tensor_tensor(y1l[:, k, :], xbuf[:, k, :], y1h[:, k, :], ALU.subtract),
                       inc="sV")

        s_ = wload([(v_f32(C, NE), GWT.rearrange("(c p) j -> p c j", p=128))])
        mkg = mm_phase(
            [{"mm": [(v_f32(C, NE)(s_)[:, k, :], xbuf[:, k, :]) for k in range(C)],
              "out": lambda b: b[0:NE, :]}],
            waits=[("sLD", slot_ld[s_]), ("sV", v_y1)], banks=[4])
        slot_mark(s_, mkg[0])
        e_g = evict(g_sb[:], 4, mkg[0], psum_ap=ps[4][0:NE, :])
        wait("tensor", "sS", e_g)
        waits_merged("tensor", [bank_free[5]])
        tmarks = []
        for b4 in range(4):
            tmarks.append(op("tensor", lambda e, b4=b4:
                             e.transpose(ps[5][0:128, b4 * 8:(b4 + 1) * 8],
                                         g_sb[0:NE, b4 * 128:(b4 + 1) * 128],
                                         ident[0:NE, 0:NE]), inc="sT"))
        e_tm = evict(sc_tm[:], 5, tmarks[-1], psum_ap=ps[5][0:128, 0:32])
        wait("vector", "sS", e_tm)
        S8 = lambda t_, b4: t_[:, b4 * 8:(b4 + 1) * 8]
        def step(fns):
            n = None
            for f in fns:
                n = op("vector", f, inc="sV")
            wait("vector", "sV", n)
        step([lambda e, b4=b4: e.tensor_reduce(mx4[:, b4:b4 + 1], S8(sc_tm, b4), AX.X, ALU.max)
              for b4 in range(4)])
        v_nm4 = vhard(lambda e: e.tensor_scalar(nmx4[:], mx4[:], -1.0, None, ALU.mult))
        wait("scalar", "sV", v_nm4)
        s_ex = None
        for b4 in range(4):
            s_ex = op("scalar", lambda e, b4=b4:
                      e.activation(S8(ex_tm, b4), S8(sc_tm, b4), AF.Exp, bias=nmx4[:, b4:b4 + 1]),
                      inc="sS")
        wait("vector", "sS", s_ex)
        step([lambda e, b4=b4: e.tensor_reduce(sm4[:, b4:b4 + 1], S8(ex_tm, b4), AX.X, ALU.add)
              for b4 in range(4)])
        vhard(lambda e: e.reciprocal(rs4[:], sm4[:]))
        step([lambda e, b4=b4:
              e.tensor_scalar(S8(ex_tm, b4), S8(ex_tm, b4), rs4[:, b4:b4 + 1], None, ALU.mult)
              for b4 in range(4)])
        step([lambda e, b4=b4: e.tensor_tensor(S8(vv, b4), S8(ex_tm, b4), gbB[:], ALU.add)
              for b4 in range(4)])
        step([lambda e, b4=b4: e.tensor_reduce(m14[:, b4:b4 + 1], S8(vv, b4), AX.X, ALU.max)
              for b4 in range(4)])
        step([lambda e, b4=b4:
              e.tensor_scalar(S8(ge1, b4), S8(vv, b4), m14[:, b4:b4 + 1], None, ALU.is_ge)
              for b4 in range(4)])
        step([lambda e, b4=b4:
              e.tensor_scalar(S8(bigb, b4), S8(ge1, b4), -BIG, None, ALU.mult)
              for b4 in range(4)])
        step([lambda e, b4=b4: e.tensor_tensor(S8(bigb, b4), S8(vv, b4), S8(bigb, b4), ALU.add)
              for b4 in range(4)])
        step([lambda e, b4=b4: e.tensor_reduce(m24[:, b4:b4 + 1], S8(bigb, b4), AX.X, ALU.max)
              for b4 in range(4)])
        step([lambda e, b4=b4:
              e.tensor_scalar(S8(keep, b4), S8(vv, b4), m24[:, b4:b4 + 1], None, ALU.is_ge)
              for b4 in range(4)])
        v_gt = None
        for b4 in range(4):
            v_gt = op("vector", lambda e, b4=b4:
                      e.tensor_tensor(S8(gts, b4), S8(ex_tm, b4), S8(keep, b4), ALU.mult), inc="sV")
        wait("tensor", "sV", v_gt)
        waits_merged("tensor", [bank_free[6]])
        tmarks = []
        for b4 in range(4):
            tmarks.append(op("tensor", lambda e, b4=b4:
                             e.transpose(ps[6][0:NE, b4 * 128:(b4 + 1) * 128],
                                         S8(gts, b4), ident[0:128, 0:128]), inc="sT"))
        e_gf = evict(gf_sb[:], 6, tmarks[-1], psum_ap=ps[6][0:NE, 0:TT])

        kb16t = kbuf[:].rearrange("p c t -> p (c t)").bitcast(BF16)
        hh = kb16t[:, 0:4096].rearrange("p (c j) -> p c j", j=512)
        hl = kb16t[:, 4096:8192].rearrange("p (c j) -> p c j", j=512)

        def mat_aps(c, which):
            if c < 2:
                hi, lo = {1: (S1HI, S1LO), 3: (S3HI, S3LO), 2: (S2HI, S2LO)}[which]
                if which in (1, 3):
                    return hi[:, c * D:(c + 1) * D], lo[:, c * D:(c + 1) * D]
                return hi[c * D:(c + 1) * D, :], lo[c * D:(c + 1) * D, :]
            e_ = c - 2
            hi, lo = {1: (W1HI, W1LO), 3: (W3HI, W3LO), 2: (W2HI, W2LO)}[which]
            return hi[e_ * D:(e_ + 1) * D, :], lo[e_ * D:(e_ + 1) * D, :]

        def ffn_unit(c, which, half, rhs_h, rhs_l, rhs_waits, extra_groups=(), extra_banks=()):
            hi_d, lo_d = mat_aps(c, which)
            colsl = slice(half * 512, (half + 1) * 512)
            s2 = wload([
                (v_hi, hi_d[:, colsl].rearrange("(c p) j -> p c j", p=128)),
                (v_lo, lo_d[:, colsl].rearrange("(c p) j -> p c j", p=128)),
            ])
            banks = [4 * half + i for i in range(4)] + list(extra_banks)
            groups = []
            for m in range(4):
                mm = []
                for k in range(C):
                    mm.append((v_hi(s2)[:, k, m * 128:(m + 1) * 128], rhs_h[:, k, :]))
                    mm.append((v_hi(s2)[:, k, m * 128:(m + 1) * 128], rhs_l[:, k, :]))
                    mm.append((v_lo(s2)[:, k, m * 128:(m + 1) * 128], rhs_h[:, k, :]))
                groups.append({"mm": mm})
            groups = groups + list(extra_groups)
            mks2 = mm_phase(groups, waits=[("sLD", slot_ld[s2])] + list(rhs_waits), banks=banks)
            slot_mark(s2, mks2[-1])
            return mks2

        y1_w = [("sS", s_y1h), ("sV", v_y1l)]
        h_ready = [("sS", 0), ("sV", 0)]
        w2_last_mark = ln1_sq_mark
        geb_evs = {}

        for cyc in range(11):
            c = cyc
            cw2 = cyc - 1
            hguard = v_y1l if cyc == 0 else cnt["sV"]
            if c <= 9:
                for half in range(2):
                    mks2 = ffn_unit(c, 1, half, y1h, y1l, y1_w)
                    for i, m in enumerate(range(4 * half, 4 * half + 4)):
                        b = 4 * half + i
                        waits_merged("scalar", [("sT", mks2[i]), ("sV", hguard)])
                        n_ = op("scalar", lambda e, m=m, b=b:
                                e.activation(h1buf[:, m, :], ps[b][:], AF.Silu), inc="sS")
                        bank_free[b] = ("sS", n_)
                for half in range(2):
                    mks2 = ffn_unit(c, 3, half, y1h, y1l, y1_w)
                    for i, m in enumerate(range(4 * half, 4 * half + 4)):
                        evict(xbuf[:, m, :], 4 * half + i, mks2[i], waits=[("sV", hguard)])
                ev_h13 = cnt["sS"]
            if cw2 >= 0:
                isexp = cw2 >= 2
                for half in range(2):
                    eg, eb = (), ()
                    if isexp and half == 0:
                        ex = cw2 - 2
                        eg = [{"mm": [(sele_sb[0:NE, ex * 128:(ex + 1) * 128], gf_sb[0:NE, :])]}]
                        eb = [7]
                    mks2 = ffn_unit(cw2, 2, half, hh, hl,
                                    list(h_ready) + ([("sS", e_gf)] if isexp else []),
                                    extra_groups=eg, extra_banks=eb)
                    if isexp and half == 0:
                        geb_evs[cw2] = evict(geb[:, cw2 % 2, :], 7, mks2[4])
                    if isexp:
                        par = cw2 % 2
                        wait("vector", "sS", geb_evs[cw2])
                        for i, m in enumerate(range(4 * half, 4 * half + 4)):
                            b = 4 * half + i
                            wait("vector", "sT", mks2[i])
                            n_ = op("vector", lambda e, i=i, b=b, par=par:
                                    e.tensor_tensor(tmpc[:, i, :], ps[b][:], geb[:, par, :], ALU.mult),
                                    inc="sV")
                            bank_free[b] = ("sV", n_)
                        for i, m in enumerate(range(4 * half, 4 * half + 4)):
                            op("vector", lambda e, i=i, m=m:
                               e.tensor_tensor(r1buf[:, m, :], r1buf[:, m, :], tmpc[:, i, :], ALU.add))
                    else:
                        for i, m in enumerate(range(4 * half, 4 * half + 4)):
                            b = 4 * half + i
                            wait("vector", "sT", mks2[i])
                            if cw2 == 0:
                                wait("vector", "sV", v_init)
                            n_ = op("vector", lambda e, m=m, b=b:
                                    e.tensor_tensor(r1buf[:, m, :], r1buf[:, m, :], ps[b][:], ALU.add),
                                    inc="sV")
                            bank_free[b] = ("sV", n_)
                    if half == 1:
                        w2_last_mark = mks2[3]
                v_acc_last = cnt["sV"]
            if c <= 9:
                wait("vector", "sS", ev_h13)
                v_h = None
                for m in range(C):
                    v_h = op("vector", lambda e, m=m:
                             e.tensor_tensor(h1buf[:, m, :], h1buf[:, m, :], xbuf[:, m, :], ALU.mult),
                             inc="sV")
                waits_merged("scalar", [("sT", w2_last_mark), ("sV", v_h)])
                s_hh = op("scalar", lambda e: e.copy(hh[:, :, :], h1buf[:]), inc="sS")
                wait("vector", "sS", s_hh)
                v_hl = None
                for m in range(C):
                    v_hl = op("vector", lambda e, m=m:
                              e.tensor_tensor(hl[:, m, :], h1buf[:, m, :], hh[:, m, :], ALU.subtract),
                              inc="sV")
                h_ready = [("sS", s_hh), ("sV", v_hl)]

        v_out, _ = layernorm(r1buf, qbuf, v_acc_last, sq_tensor_ok=w2_last_mark)

        smark = dma("gpsimd", YF.rearrange("(c p) t -> p c t", p=128)[:, :, tok], qbuf[:],
                    inc="sST", waits=[("sV", v_out)])
        store_marks.append(smark)

    wait("gpsimd", "sST", cnt["sST"])

    with nc.Block() as block:
        @block.sync
        def _(e):
            for fn in stream["sync"]:
                fn(e)

        @block.tensor
        def _(e):
            for fn in stream["tensor"]:
                fn(e)

        @block.vector
        def _(e):
            for fn in stream["vector"]:
                fn(e)

        @block.scalar
        def _(e):
            for fn in stream["scalar"]:
                fn(e)

        @block.gpsimd
        def _(e):
            for fn in stream["gpsimd"]:
                fn(e)

    return nc


def _sele():
    s = np.zeros((NE, NE * 128), dtype=np.float32)
    for e in range(NE):
        s[e, e * 128:(e + 1) * 128] = 1.0
    return s


def kernel(**inputs):
    g = {k: np.asarray(v) for k, v in inputs.items()}
    x = g["L0_attn_x"]
    B = x.shape[0]

    def cc(a):
        return np.ascontiguousarray(a.astype(np.float32, copy=False))

    shared = {
        "WqT": cc(g["L0_attn_Wq"].T), "WkT": cc(g["L0_attn_Wk"].T), "WoT": cc(g["L0_attn_Wo"].T),
        "Vb": cc(g["L0_attn_V_b"]), "Wb": cc(g["L0_attn_W_b"]),
        "Vt": cc(g["L0_attn_V_t"]), "Wt": cc(g["L0_attn_W_t"]), "Xt": cc(g["L0_attn_X_t"]),
        "UbT": cc(g["L0_attn_U_b"].T), "UtT": cc(g["L0_attn_U_t"].T),
        "BIASB": cc(g["L0_attn_bias_b"].reshape(C, 128).T),
        "BIAST": cc(g["L0_attn_bias_t"].reshape(C, 128).T),
        "ALPHA": cc(g["L0_attn_alpha"].reshape(1, 1)),
        "GWT": cc(g["L0_ffn_gate_w"].T),
        "GBIAS": cc(g["L0_ffn_gate_bias"].reshape(1, NE)),
        "W1T": cc(g["L0_ffn_w1"].transpose(0, 2, 1).reshape(NE * D, D)),
        "W3T": cc(g["L0_ffn_w3"].transpose(0, 2, 1).reshape(NE * D, D)),
        "W2T": cc(g["L0_ffn_w2"].transpose(0, 2, 1).reshape(NE * D, D)),
        "SW1T": cc(g["L0_ffn_sw1"].T), "SW3T": cc(g["L0_ffn_sw3"].T),
        "SW2T": cc(g["L0_ffn_sw2"].T),
        "IC": (1.0 / np.arange(1, T + 1, dtype=np.float64)).astype(np.float32).reshape(1, T),
        "IDENT": np.eye(128, dtype=np.float32),
        "SELE": _sele(),
    }
    in_maps = []
    for b in range(B):
        m = dict(shared)
        m["XF"] = cc(x[b].T)
        in_maps.append(m)

    nc = build_nc()
    res = run_bass_kernel_spmd(nc, in_maps, core_ids=list(range(B)))
    out = np.stack([np.asarray(r["YF"]).T for r in res.results])
    return np.ascontiguousarray(out.astype(np.float32))
